# version 01
# baseline (speedup 1.0000x reference)
import numpy as np
import ml_dtypes
import concourse.bass as bass
import concourse.mybir as mybir
from concourse.bass_utils import run_bass_kernel_spmd

F32 = mybir.dt.float32
BF16 = mybir.dt.bfloat16
AF = mybir.ActivationFunctionType
ALU = mybir.AluOpType
AX = mybir.AxisListType

S = 2048
NSEQ = 2
NT = S * NSEQ
D = 1024
DT = 8
NH = 16
HD = 64
NE = 8
DFF = 2048
FT = 16
TC = 512
LN_EPS = 1e-5
RMS_EPS = 1e-6
NEG = -1.0e30
BF = ml_dtypes.bfloat16


def host_aux():
    aux = {}
    aux["eye32"] = np.eye(128, dtype=np.float32)
    tri = np.where(np.arange(128)[None, :] > np.arange(128)[:, None],
                   np.float32(NEG), np.float32(0.0)).astype(np.float32)
    t4 = np.zeros((128, 4 * 512), dtype=np.float32)
    for v in range(4):
        t4[:, v * 512 + v * 128: v * 512 + v * 128 + 128] = tri
    aux["tri4"] = t4
    hd = np.float32(HD)
    inv_freq = (np.float32(1.0) /
                np.power(np.float32(10000.0),
                         np.arange(0, HD, 2, dtype=np.float32) / hd))
    tpos = np.arange(S, dtype=np.float32)
    freqs = tpos[:, None] * inv_freq[None, :]
    cos = np.cos(freqs).astype(np.float32)
    sin = np.sin(freqs).astype(np.float32)
    aux["cos128"] = np.ascontiguousarray(np.tile(cos.T, (4, 1)))
    aux["sin128"] = np.ascontiguousarray(
        np.concatenate([-sin.T, sin.T, -sin.T, sin.T], axis=0))
    sel = np.zeros((8, 8 * 128), dtype=BF)
    for e in range(8):
        sel[e, e * 128:(e + 1) * 128] = 1.0
    aux["sel8"] = sel
    fc = np.zeros((128, 8), dtype=np.float32)
    fc[:, 0] = -1.0
    fc[:, 1] = LN_EPS
    fc[:, 2] = RMS_EPS
    aux["fcol"] = fc
    return aux


AUX_SPECS = [("eye32", (128, 128), np.float32), ("tri4", (128, 2048), np.float32),
             ("cos128", (128, 2048), np.float32), ("sin128", (128, 2048), np.float32),
             ("sel8", (8, 1024), BF), ("fcol", (128, 8), np.float32)]


class Ctx:
    def __init__(self, nc, stack, blk):
        self.nc, self.stack, self.blk = nc, stack, blk
        self.sem, self.cnt = {}, {}
        self._sid = 0

    def new_sems(self):
        if self.sem:
            self.phase_barrier()
        for n in ("T", "V", "A", "DS", "DG"):
            self._sid += 1
            self.sem[n] = self.stack.enter_context(
                self.nc.semaphore(f"s{self._sid}{n}"))
            self.cnt[n] = 0

    def phase_barrier(self):
        if not hasattr(self, "_ph"):
            self._ph = self.stack.enter_context(self.nc.semaphore("PHB"))
            self._phv = 0
        blk = self.blk

        @blk.sync
        def _(sync):
            self.w(sync, "DS")
            sync.nop().then_inc(self._ph, 1)

        @blk.gpsimd
        def _(g):
            self.w(g, "DG")
            g.nop().then_inc(self._ph, 1)

        @blk.tensor
        def _(t):
            t.nop().then_inc(self._ph, 1)

        @blk.vector
        def _(v):
            v.nop().then_inc(self._ph, 1)

        @blk.scalar
        def _(sc):
            sc.nop().then_inc(self._ph, 1)
        self._phv += 5
        for engname in ("sync", "gpsimd", "tensor", "vector", "scalar"):
            @getattr(blk, engname)
            def _(e):
                e.wait_ge(self._ph, self._phv)

    def inc(self, instr, k, n=1):
        instr.then_inc(self.sem[k], n)
        self.cnt[k] += n

    def w(self, eng, *keys):
        for k in keys:
            if self.cnt[k] > 0:
                eng.wait_ge(self.sem[k], self.cnt[k])


def alloc_consts(c):
    sb = lambda n, s, d: c.stack.enter_context(c.nc.sbuf_tensor(n, s, d))
    cst = {
        "eye32": sb("c_eye32", [128, 128], F32),
        "eyeb": sb("c_eyeb", [128, 128], BF16),
        "tri4": sb("c_tri4", [128, 2048], F32),
        "cosb": sb("c_cosb", [128, 2048], BF16),
        "sinb": sb("c_sinb", [128, 2048], BF16),
        "sel8": sb("c_sel8", [8, 1024], BF16),
        "ones1": sb("c_ones1", [1, 128], F32),
        "eights": sb("c_eights", [1, 128], F32),
        "onescol": sb("c_onescol", [128, 1], F32),
        "onescol_b": sb("c_onescolb", [128, 1], BF16),
        "fcol": sb("c_fcol", [128, 8], F32),
    }
    return cst


def load_consts(c, io, cst, scratchF):
    blk = c.blk
    c.new_sems()

    @blk.sync
    def _(sync):
        c.inc(sync.dma_start(out=cst["eye32"][:], in_=io["eye32"][:]), "DS", 16)
        c.inc(sync.dma_start(out=cst["tri4"][:], in_=io["tri4"][:]), "DS", 16)
        c.inc(sync.dma_start(out=cst["sel8"][:], in_=io["sel8"][:]), "DS", 16)
        c.inc(sync.dma_start(out=cst["fcol"][:], in_=io["fcol"][:]), "DS", 16)
        c.inc(sync.dma_start(out=scratchF[:, 0:2048], in_=io["cos128"][:]), "DS", 16)
        c.inc(sync.dma_start(out=scratchF[:, 2048:4096], in_=io["sin128"][:]), "DS", 16)

    @blk.vector
    def _(v):
        c.w(v, "DS")
        v.memset(cst["ones1"][:], 1.0)
        v.memset(cst["eights"][:], 8.0)
        v.memset(cst["onescol"][:], 1.0)
        v.memset(cst["onescol_b"][:], 1.0)
        v.tensor_copy(cst["eyeb"][:], cst["eye32"][:])
        v.tensor_copy(cst["cosb"][:], scratchF[:, 0:2048])
        c.inc(v.tensor_copy(cst["sinb"][:], scratchF[:, 2048:4096]), "V")


def emit_in_phase(c, cst, x_nat, rA, xhat):
    blk, nc = c.blk, c.nc
    c.new_sems()
    with (
        nc.sbuf_tensor("in_nat", [128, D], F32) as natb,
        nc.sbuf_tensor("in_tx", [128, DT * 128], F32) as txb,
        nc.sbuf_tensor("in_txb", [128, DT * 128], BF16) as txbb,
        nc.psum_tensor("in_ps", [128, 512], F32) as psT,
        nc.psum_tensor("in_ps2", [128, 512], F32) as psT2,
    ):
        for g in range(NT // 128):
            s, t0 = divmod(g * 128, S)

            @blk.sync
            def _(sync):
                c.w(sync, "V", "DS")
                c.inc(sync.dma_start(out=natb[:], in_=x_nat[s, t0:t0 + 128, :]),
                      "DS", 16)

            @blk.tensor
            def _(t):
                c.w(t, "DS", "A")
                for dt in range(DT):
                    ps = psT if dt < 4 else psT2
                    c.inc(t.transpose(ps[:, (dt % 4) * 128:(dt % 4) * 128 + 128],
                                      natb[:, dt * 128:dt * 128 + 128],
                                      cst["eye32"][:]), "T")

            @blk.scalar
            def _(sc):
                c.w(sc, "T")
                sc.copy(txb[:, 0:512], psT[:, 0:512])
                c.inc(sc.copy(txb[:, 512:1024], psT2[:, 0:512]), "A")

            @blk.vector
            def _(v):
                c.w(v, "A")
                c.inc(v.tensor_copy(txbb[:], txb[:]), "V")

            @blk.sync
            def _(sync):
                c.w(sync, "V")
                for dt in range(DT):
                    c.inc(sync.dma_start(
                        out=rA[dt, :, g * 128:(g + 1) * 128],
                        in_=txb[:, dt * 128:dt * 128 + 128]), "DS", 16)
                    c.inc(sync.dma_start(
                        out=xhat[dt, :, g * 128:(g + 1) * 128],
                        in_=txbb[:, dt * 128:dt * 128 + 128]), "DS", 16)

        @blk.sync
        def _(sync):
            c.w(sync, "DS")


def emit_store_phase(c, cst, rsrc, y_nat):
    blk, nc = c.blk, c.nc
    c.new_sems()
    with (
        nc.sbuf_tensor("st_tx", [128, DT * 128], F32) as txb,
        nc.sbuf_tensor("st_nat", [128, D], F32) as natb,
        nc.psum_tensor("st_ps", [128, 512], F32) as psT,
        nc.psum_tensor("st_ps2", [128, 512], F32) as psT2,
    ):
        for g in range(NT // 128):
            s, t0 = divmod(g * 128, S)

            @blk.sync
            def _(sync):
                c.w(sync, "T")
                for dt in range(DT):
                    c.inc(sync.dma_start(
                        out=txb[:, dt * 128:dt * 128 + 128],
                        in_=rsrc[dt, :, g * 128:(g + 1) * 128]), "DS", 16)

            @blk.tensor
            def _(t):
                c.w(t, "DS", "A")
                for dt in range(DT):
                    ps = psT if dt < 4 else psT2
                    c.inc(t.transpose(ps[:, (dt % 4) * 128:(dt % 4) * 128 + 128],
                                      txb[:, dt * 128:dt * 128 + 128],
                                      cst["eye32"][:]), "T")

            @blk.scalar
            def _(sc):
                c.w(sc, "T")
                sc.copy(natb[:, 0:512], psT[:, 0:512])
                c.inc(sc.copy(natb[:, 512:1024], psT2[:, 0:512]), "A")

            @blk.sync
            def _(sync):
                c.w(sync, "A")
                c.inc(sync.dma_start(out=y_nat[s, t0:t0 + 128, :], in_=natb[:]),
                      "DS", 16)

        @blk.sync
        def _(sync):
            c.w(sync, "DS")


def emit_tail_chunk(c, cst, B, n_out_cols, u_view, cfg, s, tc, xhat, rdst):
    blk = c.blk
    t0 = s * S + tc * 512
    tb1, tb2 = B["tb1"], B["tb2"]
    row1, row2, row3, row4 = B["row1"], B["row2"], B["row3"], B["row4"]
    trep1, trep2 = B["trep1"], B["trep2"]
    psR, psRp = B["psR"], B["psRp"]
    xw = B["xw"]

    def tb(dt):
        buf = tb1 if dt < 4 else tb2
        return buf[:, (dt % 4) * 512:(dt % 4) * 512 + 512]


    if cfg["ln"]:
        @blk.vector
        def _(v):
            c.w(v, "A", "T")
            for dt in range(DT):
                ins = v.tensor_copy(tb(dt), u_view[:, dt * 512:dt * 512 + 512])
            c.inc(ins, "V")

        @blk.tensor
        def _(t):
            c.w(t, "V", "A")
            for dt in range(DT):
                c.inc(t.matmul(psR[0:1, 0:512], cst["onescol_b"][:],
                               tb(dt),
                               start=(dt == 0), stop=(dt == DT - 1)), "T")

        @blk.scalar
        def _(sc):
            c.w(sc, "T")
            c.inc(sc.activation(row1[0:1, 0:512], psR[0:1, 0:512], AF.Copy,
                                bias=0.0, scale=1.0 / D), "A")

        @blk.vector
        def _(v):
            c.w(v, "A")
            for dt in range(DT):
                ins = v.tensor_mul(tb(dt), u_view[:, dt * 512:dt * 512 + 512],
                                   u_view[:, dt * 512:dt * 512 + 512])
            c.inc(ins, "V")

        @blk.tensor
        def _(t):
            c.w(t, "V")
            for dt in range(DT):
                c.inc(t.matmul(psR[0:1, 0:512], cst["onescol_b"][:],
                               tb(dt),
                               start=(dt == 0), stop=(dt == DT - 1)), "T")

        @blk.scalar
        def _(sc):
            c.w(sc, "T")
            c.inc(sc.activation(row2[0:1, 0:512], psR[0:1, 0:512], AF.Copy,
                                bias=0.0, scale=1.0 / D), "A")

        @blk.vector
        def _(v):
            c.w(v, "A")
            c.inc(v.tensor_mul(row3[0:1, 0:512], row1[0:1, 0:512],
                               row1[0:1, 0:512]), "V")
            c.w(v, "V")
            c.inc(v.tensor_sub(row4[0:1, 0:512], row2[0:1, 0:512],
                               row3[0:1, 0:512]), "V")

        @blk.scalar
        def _(sc):
            c.w(sc, "V")
            c.inc(sc.activation(row3[0:1, 0:512], row4[0:1, 0:512], AF.Ln,
                                bias=cst["fcol"][0:1, 1:2], scale=1.0), "A")
            c.w(sc, "A")
            c.inc(sc.activation(row4[0:1, 0:512], row3[0:1, 0:512], AF.Exp,
                                bias=0.0, scale=-0.5), "A")

        @blk.tensor
        def _(t):
            c.w(t, "A", "V")
            c.inc(t.matmul(psRp[:, 0:512], cst["ones1"][:], row1[0:1, 0:512],
                           start=True, stop=True), "T")

        @blk.vector
        def _(v):
            c.w(v, "T")
            c.inc(v.tensor_copy(trep1[:, 0:512], psRp[:, 0:512]), "V")

        @blk.tensor
        def _(t):
            c.w(t, "V")
            c.inc(t.matmul(psRp[:, 0:512], cst["ones1"][:], row4[0:1, 0:512],
                           start=True, stop=True), "T")

        @blk.vector
        def _(v):
            c.w(v, "T")
            c.inc(v.tensor_copy(trep2[:, 0:512], psRp[:, 0:512]), "V")

        if B.get("dbg_tail"):
            _dbg_dump(c, "t_mu", row1[0:1, 0:512], 512)
            _dbg_dump(c, "t_msq", row2[0:1, 0:512], 512)
            _dbg_dump(c, "t_var", row4[0:1, 0:512] if False else row3[0:1, 0:512], 512)
            _dbg_dump(c, "t_rstd", row4[0:1, 0:512], 512)
            _dbg_dump(c, "t_rep1", trep1[:, 0:512], 512)
            _dbg_dump(c, "t_rep2", trep2[:, 0:512], 512)
            B["dbg_tail"] = False
        @blk.vector
        def _(v):
            for dt in range(DT):
                sl = slice(dt * 512, dt * 512 + 512)
                c.inc(v.tensor_sub(B["scr"][:, 0:512], u_view[:, sl],
                                   trep1[:, 0:512]), "V")
                c.w(v, "V")
                c.inc(v.tensor_mul(u_view[:, sl], B["scr"][:, 0:512],
                                   trep2[:, 0:512]), "V")
                c.w(v, "V")

    if cfg.get("store", True):
        @blk.sync
        def _(sync):
            c.w(sync, "V", "A")
            for dt in range(DT):
                c.inc(sync.dma_start(
                    out=rdst[dt, :, t0:t0 + 512],
                    in_=u_view[:, dt * 512:dt * 512 + 512]), "DS", 16)

    mode = cfg["xhat"]
    if mode == "plain":
        @blk.vector
        def _(v):
            c.w(v, "V")
            for dt in range(DT):
                ins = v.tensor_copy(tb(dt),
                                    u_view[:, dt * 512:dt * 512 + 512])
            c.inc(ins, "V")
    elif mode == "rms":
        @blk.vector
        def _(v):
            c.w(v, "V")
            for dt in range(DT):
                ins = v.tensor_mul(tb(dt), u_view[:, dt * 512:dt * 512 + 512],
                                   u_view[:, dt * 512:dt * 512 + 512])
            c.inc(ins, "V")

        @blk.tensor
        def _(t):
            c.w(t, "V", "A")
            for dt in range(DT):
                c.inc(t.matmul(psR[0:1, 0:512], cst["onescol_b"][:],
                               tb(dt),
                               start=(dt == 0), stop=(dt == DT - 1)), "T")

        @blk.scalar
        def _(sc):
            c.w(sc, "T")
            c.inc(sc.activation(row3[0:1, 0:512], psR[0:1, 0:512], AF.Ln,
                                bias=cst["fcol"][0:1, 2:3], scale=1.0 / D),
                  "A")
            c.w(sc, "A")
            c.inc(sc.activation(row4[0:1, 0:512], row3[0:1, 0:512], AF.Exp,
                                bias=0.0, scale=-0.5), "A")

        @blk.tensor
        def _(t):
            c.w(t, "A", "V")
            c.inc(t.matmul(psRp[:, 0:512], cst["ones1"][:], row4[0:1, 0:512],
                           start=True, stop=True), "T")

        @blk.vector
        def _(v):
            c.w(v, "T")
            c.inc(v.tensor_copy(trep2[:, 0:512], psRp[:, 0:512]), "V")

        @blk.vector
        def _(v):
            for dt in range(DT):
                sl = slice(dt * 512, dt * 512 + 512)
                c.inc(v.tensor_mul(B["scr"][:, 0:512], u_view[:, sl],
                                   trep2[:, 0:512]), "V")
                c.w(v, "V")
                c.inc(v.tensor_scalar(tb(dt), B["scr"][:, 0:512],
                                      xw[:, dt:dt + 1], None, ALU.mult), "V")
                c.w(v, "V")

    if mode is not None:
        @blk.sync
        def _(sync):
            c.w(sync, "V")
            for dt in range(DT):
                c.inc(sync.dma_start(
                    out=xhat[dt, :, t0:t0 + 512],
                    in_=tb(dt)), "DS", 16)


DBG = {"on": False, "io": None}


def _dbg_dump(c, name, ap_src, cols):
    if not DBG["on"] or name not in DBG["io"]:
        return
    blk = c.blk

    @blk.sync
    def _(sync):
        c.w(sync, "V", "A", "T")
        np_ = ap_src.shape[0]
        c.inc(sync.dma_start(out=DBG["io"][name][0:np_, 0:cols],
                             in_=ap_src), "DS", 16)
    for engname in ("sync", "gpsimd", "tensor", "vector", "scalar"):
        @getattr(blk, engname)
        def _(e):
            c.w(e, "DS")


_NWN = [0]


def load_norm_w_cols(c, cst, B, w_ap):
    blk, nc = c.blk, c.nc
    _NWN[0] += 1
    with nc.sbuf_tensor(f"nwn{_NWN[0]}", [8, 128], F32) as wn:
        @blk.sync
        def _(sync):
            c.w(sync, "T", "V")
            c.inc(sync.dma_start(out=wn[:],
                                 in_=w_ap.rearrange("(a b) -> a b", b=128)),
                  "DS", 16)

        @blk.tensor
        def _(t):
            c.w(t, "DS", "V")
            c.inc(t.transpose(B["psRp"][:, 0:8], wn[:],
                              cst["eye32"][0:8, 0:8]), "T")

        @blk.vector
        def _(v):
            c.w(v, "T")
            c.inc(v.tensor_copy(B["xw"][:, 0:8], B["psRp"][:, 0:8]), "V")


def emit_softmax_attention(c, cst, io, cfg, rsrc, rdst, xhat_in, xhat_out):
    blk, nc = c.blk, c.nc
    c.new_sems()
    name = cfg["name"]
    wqkv = io[cfg["wqkv"]]
    wproj = io[cfg["wproj"]]
    causal = cfg["causal"]
    rsub = cfg["rot_sign"] == "B"
    ptypes = cfg["ptypes"]

    _st = c.stack.__class__()
    with _st:
        xbf = _st.enter_context(nc.sbuf_tensor(f"{name}_xbf", [128, DT * S], BF16))
        aT = _st.enter_context(nc.sbuf_tensor(f"{name}_aT", [128, DT * S], BF16))
        qT = _st.enter_context(nc.sbuf_tensor(f"{name}_qT", [128, S], BF16))
        kT = _st.enter_context(nc.sbuf_tensor(f"{name}_kT", [128, S], BF16))
        vb = _st.enter_context(nc.sbuf_tensor(f"{name}_vb", [128, S], BF16))
        qb = _st.enter_context(nc.sbuf_tensor(f"{name}_qb", [128, S], BF16))
        kb = _st.enter_context(nc.sbuf_tensor(f"{name}_kb", [128, S], BF16))
        sqb = _st.enter_context(nc.sbuf_tensor(f"{name}_sq", [128, S], BF16))
        Pb = _st.enter_context(nc.sbuf_tensor(f"{name}_Pb", [128, S], BF16))
        Pn = _st.enter_context(nc.sbuf_tensor(f"{name}_Pn", [128, S], BF16))
        PT = _st.enter_context(nc.sbuf_tensor(f"{name}_PT", [128, 2048], BF16))
        vnat = _st.enter_context(nc.sbuf_tensor(f"{name}_vn", [128, S], BF16))
        arF = _st.enter_context(nc.sbuf_tensor(f"{name}_arF", [128, 6144], F32))
        repf = _st.enter_context(nc.sbuf_tensor(f"{name}_rf", [128, 2048], F32))
        wbuf = _st.enter_context(nc.sbuf_tensor(f"{name}_wb", [128, 8192], BF16))
        natb = _st.enter_context(nc.sbuf_tensor(f"{name}_nat", [128, 1024], BF16))
        row1 = _st.enter_context(nc.sbuf_tensor(f"{name}_rw1", [1, S], F32))
        row2 = _st.enter_context(nc.sbuf_tensor(f"{name}_rw2", [1, S], F32))
        row3 = _st.enter_context(nc.sbuf_tensor(f"{name}_rw3", [1, S], F32))
        row4 = _st.enter_context(nc.sbuf_tensor(f"{name}_rw4", [1, S], F32))
        cols = _st.enter_context(nc.sbuf_tensor(f"{name}_cols", [128, 40], F32))
        xw = _st.enter_context(nc.sbuf_tensor(f"{name}_xw", [128, 8], F32))
        nmcol = _st.enter_context(nc.sbuf_tensor(f"{name}_nm", [128, 1], F32))
        lcol = _st.enter_context(nc.sbuf_tensor(f"{name}_l", [128, 1], F32))
        rlcol = _st.enter_context(nc.sbuf_tensor(f"{name}_rl", [128, 1], F32))
        tqcol = _st.enter_context(nc.sbuf_tensor(f"{name}_tq", [128, 1], F32))
        ccol = _st.enter_context(nc.sbuf_tensor(f"{name}_cc", [128, 4], F32))
        psA = _st.enter_context(nc.psum_tensor(f"{name}_psA", [128, 512], F32))
        psB = _st.enter_context(nc.psum_tensor(f"{name}_psB", [128, 512], F32))
        psPT = _st.enter_context(nc.psum_tensor(f"{name}_psPT", [128, 1024], BF16))
        psO = _st.enter_context(nc.psum_tensor(f"{name}_psO", [64, 512], F32))
        psR = _st.enter_context(nc.psum_tensor(f"{name}_psR", [1, 512], F32))
        psRp = _st.enter_context(nc.psum_tensor(f"{name}_psRp", [128, 512], F32))
        psC = _st.enter_context(nc.psum_tensor(f"{name}_psC", [128, 512], F32))
        Ssb = arF[:, 0:2048]
        Sxb = arF[:, 2048:4096]
        Syb = arF[:, 4096:6144]
        u_view = arF[:, 0:4096]
        rc_view = arF[:, 4096:6144]
        B = dict(tb1=Pb, tb2=Pn, row1=row1, row2=row2, row3=row3, row4=row4,
                 trep1=repf[:, 0:512], trep2=repf[:, 512:1024],
                 scr=repf[:, 1024:1536], psR=psR, psRp=psRp, xw=xw)

        have_bq = cfg.get("bqkv") is not None
        have_bp = cfg.get("bproj") is not None
        with nc.sbuf_tensor(f"{name}_bn", [24, 128], F32) as bn:
            for (have, apname, ncols, dstofs) in (
                    (have_bq, cfg.get("bqkv"), 24, 0),
                    (have_bp, cfg.get("bproj"), 8, 24)):
                if not have:
                    continue
                @blk.sync
                def _(sync, apname=apname, ncols=ncols):
                    c.w(sync, "T")
                    c.inc(sync.dma_start(
                        out=bn[0:ncols, :],
                        in_=io[apname].rearrange("(a b) -> a b", b=128)),
                        "DS", 16)

                @blk.tensor
                def _(t, ncols=ncols):
                    c.w(t, "DS", "V")
                    c.inc(t.transpose(psC[:, 0:ncols], bn[0:ncols, :],
                                      cst["eye32"][0:ncols, 0:ncols]), "T")

                @blk.vector
                def _(v, ncols=ncols, dstofs=dstofs):
                    c.w(v, "T")
                    c.inc(v.tensor_copy(cols[:, dstofs:dstofs + ncols],
                                        psC[:, 0:ncols]), "V")

        if cfg.get("rms_w"):
            load_norm_w_cols(c, cst, B, io[cfg["rms_w"]])


        for s in range(NSEQ):
            @blk.sync
            def _(sync, s=s):
                c.w(sync, "T", "V", "A")
                for dt in range(DT):
                    c.inc(sync.dma_start(out=xbf[:, dt * S:(dt + 1) * S],
                                         in_=xhat_in[dt, :, s * S:(s + 1) * S]),
                          "DS", 16)
            for p in range(NH // 2):
                pt = ptypes[p]
                for sec in range(3):
                    @blk.gpsimd
                    def _(g, sec=sec):
                        c.w(g, "T", "V")
                        c.inc(g.dma_start(
                            out=natb[:],
                            in_=wqkv[sec * D + p * 128: sec * D + (p + 1) * 128, :]),
                            "DG", 16)

                    @blk.tensor
                    def _(t, sec=sec):
                        c.w(t, "DG", "V")
                        for dt in range(DT):
                            c.inc(t.transpose(psPT[:, dt * 128:dt * 128 + 128],
                                              natb[:, dt * 128:dt * 128 + 128],
                                              cst["eyeb"][:]), "T")

                    @blk.vector
                    def _(v, sec=sec):
                        c.w(v, "T")
                        c.inc(v.tensor_copy(wbuf[:, sec * 1024:(sec + 1) * 1024],
                                            psPT[:]), "V")

                for sec, dst in ((0, qT), (1, kT), (2, vb)):
                    for tcq in range(4):
                        ps = psA if tcq % 2 == 0 else psB
                        @blk.tensor
                        def _(t, sec=sec, tcq=tcq, ps=ps):
                            c.w(t, "V", "A", "DS")
                            for dt in range(DT):
                                c.inc(t.matmul(
                                    ps[:, 0:512],
                                    wbuf[:, sec * 1024 + dt * 128: sec * 1024 + dt * 128 + 128],
                                    xbf[:, dt * S + tcq * 512:
                                        dt * S + tcq * 512 + 512],
                                    start=(dt == 0), stop=(dt == DT - 1)), "T")

                        if have_bq:
                            @blk.vector
                            def _(v, sec=sec, tcq=tcq, ps=ps, dst=dst):
                                c.w(v, "T")
                                c.inc(v.tensor_scalar(
                                    dst[:, tcq * 512:tcq * 512 + 512],
                                    ps[:, 0:512],
                                    cols[:, sec * 8 + p: sec * 8 + p + 1],
                                    None, ALU.add), "V")
                        else:
                            @blk.scalar
                            def _(sc, sec=sec, tcq=tcq, ps=ps, dst=dst):
                                c.w(sc, "T")
                                c.inc(sc.activation(dst[:, tcq * 512:tcq * 512 + 512],
                                                    ps[:, 0:512], AF.Copy,
                                                    bias=0.0, scale=1.0), "A")

                for src, dst in ((qT, qb), (kT, kb)):
                    @blk.vector
                    def _(v, src=src, dst=dst):
                        c.w(v, "A", "T")
                        v.tensor_mul(Pb[:], src[:], cst["cosb"][:])
                        v.tensor_mul(Pn[0:32, :], src[32:64, :], cst["sinb"][32:64, :])
                        v.tensor_mul(Pn[32:64, :], src[0:32, :], cst["sinb"][0:32, :])
                        v.tensor_mul(Pn[64:96, :], src[96:128, :], cst["sinb"][96:128, :])
                        c.inc(v.tensor_mul(Pn[96:128, :], src[64:96, :], cst["sinb"][64:96, :]), "V")
                        c.w(v, "V")
                        if rsub:
                            c.inc(v.tensor_sub(dst[:], Pb[:], Pn[:]), "V")
                        else:
                            c.inc(v.tensor_add(dst[:], Pb[:], Pn[:]), "V")

                for hh in range(2):
                    b0 = hh * 64
                    @blk.tensor
                    def _(t, hh=hh, b0=b0):
                        c.w(t, "V", "A")
                        for j in range(16):
                            c.inc(t.transpose(psPT[:, j * 64:j * 64 + 64],
                                              vb[b0:b0 + 64, j * 128:j * 128 + 128],
                                              cst["eyeb"][b0:b0 + 64, b0:b0 + 64]),
                                  "T")

                    @blk.vector
                    def _(v, hh=hh):
                        c.w(v, "T")
                        c.inc(v.tensor_copy(vnat[:, hh * 1024:(hh + 1) * 1024],
                                            psPT[:]), "V")

                if s == 0 and p == 0:
                    _dbg_dump(c, "qT", qT[:, 0:2048], 2048)
                    _dbg_dump(c, "kT", kT[:, 0:2048], 2048)
                    _dbg_dump(c, "vb", vb[:, 0:2048], 2048)
                    _dbg_dump(c, "qb", qb[:, 0:2048], 2048)
                    _dbg_dump(c, "kb", kb[:, 0:2048], 2048)
                HA = dict(cst=cst, cfg=cfg, aT=aT, Ssb=Ssb, Sxb=Sxb, Syb=Syb,
                          repf=repf, sqb=sqb, vnat=vnat, Pb=Pb, Pn=Pn, PT=PT,
                          psA=psA, psB=psB, psPT=psPT, psO=psO, psC=psC,
                          nmcol=nmcol, lcol=lcol, rlcol=rlcol, tqcol=tqcol,
                          ccol=ccol)
                HA["dbg"] = (s == 0 and p == 0)
                if pt == "sdpa":
                    for hh in range(2):
                        HA["dbg"] = (s == 0 and p == 0 and hh == 0)
                        _emit_head(c, HA, s, p, hh, "sdpa", qb, kb)
                elif pt == "cos":
                    for src, rowdst, lhs, repdst in (
                            (qb, row1, cst["eights"], Pb),
                            (kb, row2, cst["ones1"], Pn)):
                        @blk.vector
                        def _(v, src=src):
                            c.w(v, "V", "T", "A")
                            c.inc(v.tensor_mul(sqb[:], src[:], src[:]), "V")
                        for hh in range(2):
                            b0 = hh * 64
                            for tcq in range(4):
                                @blk.tensor
                                def _(t, b0=b0, tcq=tcq):
                                    c.w(t, "V", "A")
                                    c.inc(t.matmul(
                                        psR[0:1, 0:512],
                                        cst["onescol_b"][b0:b0 + 64, :],
                                        sqb[b0:b0 + 64, tcq * 512:tcq * 512 + 512],
                                        start=True, stop=True), "T")

                                @blk.scalar
                                def _(sc, rowdst=rowdst, tcq=tcq):
                                    c.w(sc, "T")
                                    c.inc(sc.copy(
                                        rowdst[0:1, tcq * 512:tcq * 512 + 512],
                                        psR[0:1, 0:512]), "A")
                            @blk.scalar
                            def _(sc, rowdst=rowdst):
                                c.w(sc, "A")
                                c.inc(sc.activation(row3[:], rowdst[:], AF.Ln),
                                      "A")
                                c.w(sc, "A")
                                c.inc(sc.activation(rowdst[:], row3[:], AF.Exp,
                                                    bias=0.0, scale=0.5), "A")

                            @blk.vector
                            def _(v, rowdst=rowdst):
                                c.w(v, "A")
                                c.inc(v.tensor_scalar(row3[:], rowdst[:],
                                                      1e-12, None, ALU.max),
                                      "V")
                                c.w(v, "V")
                                c.inc(v.reciprocal(rowdst[:], row3[:]), "V")
                            for tcq in range(4):
                                @blk.tensor
                                def _(t, lhs=lhs, rowdst=rowdst, tcq=tcq):
                                    c.w(t, "V", "A")
                                    c.inc(t.matmul(
                                        psRp[0:64, 0:512], lhs[0:1, 0:64],
                                        rowdst[0:1, tcq * 512:tcq * 512 + 512],
                                        start=True, stop=True), "T")

                                @blk.scalar
                                def _(sc, repdst=repdst, b0=b0, tcq=tcq):
                                    c.w(sc, "T")
                                    c.inc(sc.copy(
                                        repdst[b0:b0 + 64, tcq * 512:tcq * 512 + 512],
                                        psRp[0:64, 0:512]), "A")
                    @blk.vector
                    def _(v):
                        c.w(v, "A", "T")
                        v.tensor_mul(qT[:], qb[:], Pb[:])
                        c.inc(v.tensor_mul(kT[:], kb[:], Pn[:]), "V")
                    for hh in range(2):
                        _emit_head(c, HA, s, p, hh, "cos", qT, kT)
                else:
                    @blk.vector
                    def _(v):
                        c.w(v, "V", "T", "A")
                        v.tensor_mul(Pb[:], kb[:], kb[:])
                        c.inc(v.tensor_mul(sqb[:], qb[:], qb[:]), "V")
                    for hh in range(2):
                        b0 = hh * 64
                        rowK = row2 if hh == 0 else row1
                        for tcq in range(4):
                            @blk.tensor
                            def _(t, b0=b0, tcq=tcq):
                                c.w(t, "V", "A")
                                c.inc(t.matmul(
                                    psR[0:1, 0:512],
                                    cst["onescol_b"][b0:b0 + 64, :],
                                    Pb[b0:b0 + 64, tcq * 512:tcq * 512 + 512],
                                    start=True, stop=True), "T")

                            @blk.scalar
                            def _(sc, rowK=rowK, tcq=tcq):
                                c.w(sc, "T")
                                c.inc(sc.copy(rowK[0:1, tcq * 512:tcq * 512 + 512],
                                              psR[0:1, 0:512]), "A")
                    for hh in range(2):
                        b0 = hh * 64
                        h = 2 * p + hh
                        rowK = row2 if hh == 0 else row1
                        @blk.sync
                        def _(sync, h=h):
                            c.w(sync, "V", "A", "T")
                            c.inc(sync.dma_start(
                                out=ccol[:, 0:1],
                                in_=io["curv_tok"][h - 6, :]), "DS", 16)

                        @blk.vector
                        def _(v):
                            c.w(v, "DS")
                            c.inc(v.reciprocal(ccol[:, 1:2], ccol[:, 0:1]), "V")

                        @blk.scalar
                        def _(sc):
                            c.w(sc, "V")
                            c.inc(sc.activation(ccol[:, 3:4], ccol[:, 1:2],
                                                AF.Ln), "A")
                            c.w(sc, "A")
                            c.inc(sc.activation(ccol[:, 2:3], ccol[:, 3:4],
                                                AF.Exp, bias=0.0, scale=0.5),
                                  "A")
                        @blk.vector
                        def _(v, rowK=rowK):
                            c.w(v, "A")
                            c.inc(v.tensor_scalar(row3[:], rowK[:],
                                                  ccol[0:1, 1:2], None,
                                                  ALU.add), "V")

                        @blk.scalar
                        def _(sc):
                            c.w(sc, "V")
                            c.inc(sc.activation(row4[:], row3[:], AF.Ln), "A")
                            c.w(sc, "A")
                            c.inc(sc.activation(row3[:], row4[:], AF.Exp,
                                                bias=0.0, scale=0.5), "A")

                        @blk.vector
                        def _(v):
                            c.w(v, "A")
                            c.inc(v.tensor_scalar(row4[:], row3[:],
                                                  ccol[0:1, 0:1], None,
                                                  ALU.mult), "V")
                        for tcq in range(4):
                            @blk.tensor
                            def _(t, tcq=tcq):
                                c.w(t, "V", "A")
                                c.inc(t.matmul(
                                    psRp[:, 0:512], cst["ones1"][:],
                                    row4[0:1, tcq * 512:tcq * 512 + 512],
                                    start=True, stop=True), "T")

                            @blk.scalar
                            def _(sc, tcq=tcq):
                                c.w(sc, "T")
                                c.inc(sc.copy(repf[:, tcq * 512:tcq * 512 + 512],
                                              psRp[:, 0:512]), "A")
                        @blk.vector
                        def _(v, b0=b0):
                            c.w(v, "A")
                            c.inc(v.tensor_scalar(
                                qT[b0:b0 + 64, :], qb[b0:b0 + 64, :],
                                ccol[b0:b0 + 64, 0:1], None, ALU.mult), "V")
                        _emit_head(c, HA, s, p, hh, "hyp", qT, kb)

            if s == 0:
                _dbg_dump(c, "aT", aT[:, 0:2048], 2048)
                _dbg_dump(c, "aT3", aT[:, 3 * S:3 * S + 2048], 2048)
                _dbg_dump(c, "aT5", aT[:, 5 * S:5 * S + 2048], 2048)
            for ob in range(8):
                @blk.gpsimd
                def _(g, ob=ob):
                    c.w(g, "T", "V")
                    c.inc(g.dma_start(
                        out=natb[:],
                        in_=wproj[ob * 128:(ob + 1) * 128, :]), "DG", 16)

                @blk.tensor
                def _(t, ob=ob):
                    c.w(t, "DG", "V")
                    for dt in range(DT):
                        c.inc(t.transpose(psPT[:, dt * 128:dt * 128 + 128],
                                          natb[:, dt * 128:dt * 128 + 128],
                                          cst["eyeb"][:]), "T")

                @blk.vector
                def _(v, ob=ob):
                    c.w(v, "T")
                    c.inc(v.tensor_copy(wbuf[:, ob * 1024:(ob + 1) * 1024],
                                        psPT[:]), "V")

            for tc in range(4):
                @blk.sync
                def _(sync, tc=tc):
                    c.w(sync, "V", "A", "T")
                    for dt in range(DT):
                        c.inc(sync.dma_start(
                            out=rc_view[:, dt * 256:dt * 256 + 256] if False
                            else arF[:, 4096 + dt * 256: 4096 + dt * 256 + 256],
                            in_=rsrc[dt, :, s * S + tc * 512 + 0:
                                     s * S + tc * 512 + 256]), "DS", 16)
                for half in range(2):
                    if half == 1:
                        @blk.sync
                        def _(sync, tc=tc):
                            c.w(sync, "V")
                            for dt in range(DT):
                                c.inc(sync.dma_start(
                                    out=arF[:, 4096 + dt * 256: 4096 + dt * 256 + 256],
                                    in_=rsrc[dt, :, s * S + tc * 512 + 256:
                                             s * S + tc * 512 + 512]), "DS", 16)
                    for ob in range(8):
                        ps = psA if ob % 2 == 0 else psB
                        @blk.tensor
                        def _(t, tc=tc, ob=ob, half=half, ps=ps):
                            c.w(t, "V", "A", "DS")
                            for adt in range(DT):
                                c.inc(t.matmul(
                                    ps[:, 0:256],
                                    wbuf[:, ob * 1024 + adt * 128: ob * 1024 + adt * 128 + 128],
                                    aT[:, adt * S + tc * 512 + half * 256:
                                       adt * S + tc * 512 + half * 256 + 256],
                                    start=(adt == 0), stop=(adt == DT - 1)), "T")

                        @blk.vector
                        def _(v, tc=tc, ob=ob, half=half, ps=ps):
                            c.w(v, "T", "DS")
                            bias = (cols[:, 24 + ob: 24 + ob + 1] if have_bp
                                    else 0.0)
                            c.inc(v.scalar_tensor_tensor(
                                u_view[:, ob * 512 + half * 256:
                                       ob * 512 + half * 256 + 256],
                                ps[:, 0:256], bias,
                                arF[:, 4096 + ob * 256: 4096 + ob * 256 + 256],
                                ALU.add, ALU.add), "V")
                if s == 0 and tc == 0:
                    _dbg_dump(c, "u0a", u_view[:, 0:2048], 2048)
                    _dbg_dump(c, "u0b", u_view[:, 2048:4096], 2048)
                    B["dbg_tail"] = True
                emit_tail_chunk(c, cst, B, 8, u_view,
                                dict(ln=cfg["ln"], xhat=cfg["xhat"],
                                     store=True),
                                s, tc, xhat_out, rdst)

        @blk.sync
        def _(sync):
            c.w(sync, "DS", "DG", "V", "A", "T")


def _emit_head(c, HA, s, p, hh, pt, Q, K):
    blk = c.blk
    cst = HA["cst"]
    cfg = HA["cfg"]
    aT, Ssb, Sxb, Syb = HA["aT"], HA["Ssb"], HA["Sxb"], HA["Syb"]
    repf, sqb, vnat = HA["repf"], HA["sqb"], HA["vnat"]
    Pb, Pn, PT = HA["Pb"], HA["Pn"], HA["PT"]
    psA, psB, psPT, psO, psC = (HA["psA"], HA["psB"], HA["psPT"], HA["psO"],
                                HA["psC"])
    nmcol, lcol, rlcol, tqcol, ccol = (HA["nmcol"], HA["lcol"], HA["rlcol"],
                                       HA["tqcol"], HA["ccol"])
    causal = cfg["causal"]
    b0 = hh * 64
    scale = {"sdpa": 0.125, "cos": 1.0, "hyp": 1.0}[pt]
    for qt in range(16):
        k_len = (qt + 1) * 128 if causal else 2048
        nch = (k_len + 511) // 512
        if pt == "hyp":
            @blk.tensor
            def _(t, qt=qt):
                c.w(t, "V", "A")
                c.inc(t.matmul(psC[:, 0:1],
                               sqb[b0:b0 + 64, qt * 128:qt * 128 + 128],
                               cst["onescol_b"][b0:b0 + 64, :],
                               start=True, stop=True), "T")

            @blk.vector
            def _(v):
                c.w(v, "T")
                c.inc(v.tensor_scalar(tqcol[:], psC[:, 0:1], ccol[:, 1:2],
                                      None, ALU.add), "V")

            @blk.scalar
            def _(sc):
                c.w(sc, "V")
                c.inc(sc.activation(nmcol[:], tqcol[:], AF.Ln), "A")
                c.w(sc, "A")
                c.inc(sc.activation(tqcol[:], nmcol[:], AF.Exp,
                                    bias=0.0, scale=0.5), "A")

        for i in range(nch):
            cl = min(512, k_len - i * 512)
            ps = psA if i % 2 == 0 else psB
            @blk.tensor
            def _(t, qt=qt, i=i, cl=cl, ps=ps):
                c.w(t, "V", "A")
                c.inc(t.matmul(ps[:, 0:cl],
                               Q[b0:b0 + 64, qt * 128:qt * 128 + 128],
                               K[b0:b0 + 64, i * 512:i * 512 + cl],
                               start=True, stop=True), "T")
            is_diag_chunk = causal and (i == nch - 1)
            if pt == "hyp":
                @blk.vector
                def _(v, qt=qt, i=i, cl=cl, ps=ps):
                    c.w(v, "T", "A")
                    c.inc(v.scalar_tensor_tensor(
                        Ssb[:, i * 512:i * 512 + cl],
                        repf[:, i * 512:i * 512 + cl], tqcol[:],
                        ps[:, 0:cl], ALU.mult, ALU.subtract), "V")
            elif is_diag_chunk:
                v4 = qt % 4
                @blk.vector
                def _(v, qt=qt, i=i, cl=cl, ps=ps, v4=v4):
                    c.w(v, "T", "A")
                    c.inc(v.scalar_tensor_tensor(
                        Ssb[:, i * 512:i * 512 + cl],
                        ps[:, 0:cl], scale,
                        cst["tri4"][:, v4 * 512:v4 * 512 + cl],
                        ALU.mult, ALU.add), "V")
            else:
                @blk.scalar
                def _(sc, qt=qt, i=i, cl=cl, ps=ps):
                    c.w(sc, "T", "V")
                    c.inc(sc.activation(Ssb[:, i * 512:i * 512 + cl],
                                        ps[:, 0:cl], AF.Copy,
                                        bias=0.0, scale=scale), "A")
        if pt == "hyp":
            @blk.vector
            def _(v, qt=qt, k_len=k_len):
                c.w(v, "V", "A")
                kl = slice(0, k_len)
                c.inc(v.tensor_scalar(Sxb[:, kl], Ssb[:, kl], 1.0 + 1e-7,
                                      None, ALU.max), "V")
                c.w(v, "V")
                c.inc(v.tensor_mul(Ssb[:, kl], Sxb[:, kl], Sxb[:, kl]), "V")

            @blk.scalar
            def _(sc, k_len=k_len):
                c.w(sc, "V")
                kl = slice(0, k_len)
                c.inc(sc.activation(Syb[:, kl], Ssb[:, kl], AF.Ln,
                                    bias=cst["fcol"][:, 0:1]), "A")
                c.w(sc, "A")
                c.inc(sc.activation(Ssb[:, kl], Syb[:, kl], AF.Exp,
                                    bias=0.0, scale=0.5), "A")

            @blk.vector
            def _(v, k_len=k_len):
                c.w(v, "A")
                kl = slice(0, k_len)
                c.inc(v.tensor_add(Syb[:, kl], Sxb[:, kl], Ssb[:, kl]), "V")

            @blk.scalar
            def _(sc, k_len=k_len):
                c.w(sc, "V")
                kl = slice(0, k_len)
                c.inc(sc.activation(Sxb[:, kl], Syb[:, kl], AF.Ln), "A")

            @blk.vector
            def _(v, qt=qt, k_len=k_len):
                c.w(v, "A")
                kl = slice(0, k_len)
                c.inc(v.tensor_scalar(Syb[:, kl], Sxb[:, kl], ccol[:, 2:3],
                                      1e-6, ALU.mult, ALU.add), "V")
                c.w(v, "V")
                c.inc(v.reciprocal(Ssb[:, kl], Syb[:, kl]), "V")
            if causal:
                d0 = qt * 128
                @blk.vector
                def _(v, d0=d0):
                    c.inc(v.tensor_add(Sxb[:, 0:128], Ssb[:, d0:d0 + 128],
                                       cst["tri4"][:, 0:128]), "V")
                    c.w(v, "V")
                    c.inc(v.tensor_copy(Ssb[:, d0:d0 + 128], Sxb[:, 0:128]),
                          "V")

        if HA.get("dbg") and qt == 3:
            _dbg_dump(c, "Ssb", Ssb[:, 0:512], 512)
        @blk.vector
        def _(v, k_len=k_len):
            c.w(v, "V", "A")
            c.inc(v.reduce_max(nmcol[:], Ssb[:, 0:k_len], axis=AX.X,
                               negate=True), "V")

        @blk.scalar
        def _(sc, k_len=k_len):
            c.w(sc, "V")
            c.inc(sc.activation(Pb[:, 0:k_len], Ssb[:, 0:k_len], AF.Exp,
                                bias=nmcol[:], scale=1.0), "A")

        @blk.vector
        def _(v, k_len=k_len):
            c.w(v, "A", "T")
            c.inc(v.reduce_sum(lcol[:], Pb[:, 0:k_len], axis=AX.X), "V")
            c.w(v, "V")
            c.inc(v.reciprocal(rlcol[:], lcol[:]), "V")
            c.w(v, "V")
            c.inc(v.tensor_scalar(Pn[:, 0:k_len], Pb[:, 0:k_len], rlcol[:],
                                  None, ALU.mult), "V")
        if HA.get("dbg") and qt == 3:
            _dbg_dump(c, "Pn", Pn[:, 0:512], 512)

        nk = k_len // 128
        for r in range((nk + 7) // 8):
            jn = min(8, nk - r * 8)
            @blk.tensor
            def _(t, r=r, jn=jn):
                c.w(t, "V", "A")
                for jj in range(jn):
                    c.inc(t.transpose(psPT[:, jj * 128:jj * 128 + 128],
                                      Pn[:, (r * 8 + jj) * 128:(r * 8 + jj) * 128 + 128],
                                      cst["eyeb"][:]), "T")

            @blk.vector
            def _(v, r=r, jn=jn):
                c.w(v, "T")
                c.inc(v.tensor_copy(PT[:, r * 1024:r * 1024 + jn * 128],
                                    psPT[:, 0:jn * 128]), "V")

        @blk.tensor
        def _(t, qt=qt, nk=nk):
            c.w(t, "V", "A")
            for j in range(nk):
                c.inc(t.matmul(psO[0:64, 0:128],
                               vnat[:, hh * 1024 + j * 64:hh * 1024 + j * 64 + 64],
                               PT[:, j * 128:j * 128 + 128],
                               start=(j == 0), stop=(j == nk - 1)), "T")

        @blk.scalar
        def _(sc, qt=qt):
            c.w(sc, "T")
            c.inc(sc.copy(aT[b0:b0 + 64, p * S + qt * 128: p * S + qt * 128 + 128],
                          psO[0:64, 0:128]), "A")


def emit_moe(c, cst, io, cfg, rsrcs, rdst, xhat_in, xhat_out, y_nat=None):
    blk, nc = c.blk, c.nc
    c.new_sems()
    name = cfg["name"]
    wg = io[cfg["wg"]]
    w1, b1 = io[cfg["w1"]], io[cfg["b1"]]
    w2, b2 = io[cfg["w2"]], io[cfg["b2"]]
    w3, b3 = io[cfg["w3"]], io[cfg["b3"]]
    w1t, w2t, w3t = cfg["w1t"], cfg["w2t"], cfg["w3t"]

    _st = c.stack.__class__()
    with _st:
        xc = _st.enter_context(nc.sbuf_tensor(f"{name}_xc", [128, DT * TC], BF16))
        hbf = _st.enter_context(nc.sbuf_tensor(f"{name}_h", [128, FT * TC], BF16))
        acA = _st.enter_context(nc.sbuf_tensor(f"{name}_acA", [128, DT * TC], F32))
        acB = _st.enter_context(nc.sbuf_tensor(f"{name}_acB", [128, DT * TC], F32))
        w12 = _st.enter_context(nc.sbuf_tensor(f"{name}_w12", [128, 4096], BF16))
        w3s = _st.enter_context(nc.sbuf_tensor(f"{name}_w3s", [128, 4096], BF16))
        gsb = _st.enter_context(nc.sbuf_tensor(f"{name}_gs", [128, 512], F32))
        gs2 = _st.enter_context(nc.sbuf_tensor(f"{name}_gs2", [128, 512], F32))
        combrep = _st.enter_context(nc.sbuf_tensor(f"{name}_cr", [128, TC], F32))
        natb = _st.enter_context(nc.sbuf_tensor(f"{name}_nat", [128, 1024], BF16))
        arF = _st.enter_context(nc.sbuf_tensor(f"{name}_arF", [128, 6144], F32))
        arF2 = _st.enter_context(nc.sbuf_tensor(f"{name}_ar2", [128, 2048], F32))
        tb1 = _st.enter_context(nc.sbuf_tensor(f"{name}_tb1", [128, 2048], BF16))
        tb2 = _st.enter_context(nc.sbuf_tensor(f"{name}_tb2", [128, 2048], BF16))
        lgT = _st.enter_context(nc.sbuf_tensor(f"{name}_lgT", [8, TC], F32))
        comb8 = _st.enter_context(nc.sbuf_tensor(f"{name}_cb8", [8, TC], F32))
        comb8b = _st.enter_context(nc.sbuf_tensor(f"{name}_cb8b", [8, TC], BF16))
        b3nat = _st.enter_context(nc.sbuf_tensor(f"{name}_b3n", [8, 1024], BF16))
        lg = _st.enter_context(nc.sbuf_tensor(f"{name}_lg", [128, 8], F32))
        ga = _st.enter_context(nc.sbuf_tensor(f"{name}_ga", [128, 8], F32))
        gb = _st.enter_context(nc.sbuf_tensor(f"{name}_gb", [128, 8], F32))
        gc = _st.enter_context(nc.sbuf_tensor(f"{name}_gc", [128, 8], F32))
        gm1 = _st.enter_context(nc.sbuf_tensor(f"{name}_gm", [128, 1], F32))
        gm2 = _st.enter_context(nc.sbuf_tensor(f"{name}_gm2", [128, 1], F32))
        gm3 = _st.enter_context(nc.sbuf_tensor(f"{name}_gm3", [128, 1], F32))
        wgl = _st.enter_context(nc.sbuf_tensor(f"{name}_wgl", [128, 64], BF16))
        b1c = _st.enter_context(nc.sbuf_tensor(f"{name}_b1c", [128, 128], F32))
        b2c = _st.enter_context(nc.sbuf_tensor(f"{name}_b2c", [128, 128], F32))
        bn16 = _st.enter_context(nc.sbuf_tensor(f"{name}_bn16", [16, 128], F32))
        row1 = _st.enter_context(nc.sbuf_tensor(f"{name}_rw1", [1, 512], F32))
        row2 = _st.enter_context(nc.sbuf_tensor(f"{name}_rw2", [1, 512], F32))
        row3 = _st.enter_context(nc.sbuf_tensor(f"{name}_rw3", [1, 512], F32))
        row4 = _st.enter_context(nc.sbuf_tensor(f"{name}_rw4", [1, 512], F32))
        trep1 = _st.enter_context(nc.sbuf_tensor(f"{name}_tr1", [128, 512], F32))
        trep2 = _st.enter_context(nc.sbuf_tensor(f"{name}_tr2", [128, 512], F32))
        scrb = _st.enter_context(nc.sbuf_tensor(f"{name}_scr", [128, 512], F32))
        natf = _st.enter_context(nc.sbuf_tensor(f"{name}_natf", [128, 1024], F32))
        psG = _st.enter_context(nc.psum_tensor(f"{name}_psG", [128, 512], F32))
        psV = _st.enter_context(nc.psum_tensor(f"{name}_psV", [128, 512], F32))
        psY = _st.enter_context(nc.psum_tensor(f"{name}_psY", [128, 512], F32))
        psPT = _st.enter_context(nc.psum_tensor(f"{name}_psPT", [128, 1024], BF16))
        psR = _st.enter_context(nc.psum_tensor(f"{name}_psR", [1, 512], F32))
        psRp = _st.enter_context(nc.psum_tensor(f"{name}_psRp", [128, 512], F32))
        psC = _st.enter_context(nc.psum_tensor(f"{name}_psC", [128, 512], F32))
        B = dict(tb1=tb1, tb2=tb2, row1=row1, row2=row2, row3=row3, row4=row4,
                 trep1=trep1, trep2=trep2, scr=scrb, psR=psR, psRp=psRp,
                 xw=None)

        for e in range(NE):
            for (wsrc, wdst, nb) in ((w1, w1t, FT), (w2, w2t, FT)):
                for fb in range(nb):
                    @blk.gpsimd
                    def _(g, e=e, wsrc=wsrc, fb=fb):
                        c.w(g, "T", "DS")
                        c.inc(g.dma_start(
                            out=natb[:],
                            in_=wsrc[e, fb * 128:(fb + 1) * 128, :]), "DG", 16)

                    @blk.tensor
                    def _(t):
                        c.w(t, "DG", "V")
                        for dt in range(DT):
                            c.inc(t.transpose(psPT[:, dt * 128:dt * 128 + 128],
                                              natb[:, dt * 128:dt * 128 + 128],
                                              cst["eyeb"][:]), "T")

                    @blk.vector
                    def _(v):
                        c.w(v, "T")
                        c.inc(v.tensor_copy(w12[:, 0:1024], psPT[:]), "V")

                    @blk.sync
                    def _(sync, e=e, wdst=wdst, fb=fb):
                        c.w(sync, "V")
                        c.inc(sync.dma_start(out=wdst[e, fb, :, :],
                                             in_=w12[:, 0:1024]), "DS", 16)
            for db in range(DT):
                for half in range(2):
                    @blk.gpsimd
                    def _(g, e=e, db=db, half=half):
                        c.w(g, "T", "DS")
                        c.inc(g.dma_start(
                            out=natb[:],
                            in_=w3[e, db * 128:(db + 1) * 128,
                                   half * 1024:(half + 1) * 1024]), "DG", 16)

                    @blk.tensor
                    def _(t):
                        c.w(t, "DG", "V")
                        for ft in range(8):
                            c.inc(t.transpose(psPT[:, ft * 128:ft * 128 + 128],
                                              natb[:, ft * 128:ft * 128 + 128],
                                              cst["eyeb"][:]), "T")

                    @blk.vector
                    def _(v):
                        c.w(v, "T")
                        c.inc(v.tensor_copy(w12[:, 0:1024], psPT[:]), "V")

                    @blk.sync
                    def _(sync, e=e, db=db, half=half):
                        c.w(sync, "V")
                        c.inc(sync.dma_start(
                            out=w3t[e, db, :, half * 1024:(half + 1) * 1024],
                            in_=w12[:, 0:1024]), "DS", 16)

        @blk.gpsimd
        def _(g):
            c.w(g, "T", "DS")
            c.inc(g.dma_start(out=natb[0:8, :], in_=wg[:, :]), "DG", 16)

        @blk.tensor
        def _(t):
            c.w(t, "DG", "V")
            for dt in range(DT):
                c.inc(t.transpose(psPT[:, dt * 8:dt * 8 + 8],
                                  natb[0:8, dt * 128:dt * 128 + 128],
                                  cst["eyeb"][0:8, 0:8]), "T")

        @blk.vector
        def _(v):
            c.w(v, "T")
            c.inc(v.tensor_copy(wgl[:, 0:64], psPT[:, 0:64]), "V")

        @blk.gpsimd
        def _(g):
            c.inc(g.dma_start(out=b3nat[:], in_=b3[:, :]), "DG", 16)

        for e in range(NE):
            for (bsrc, bdst) in ((b1, b1c), (b2, b2c)):
                @blk.sync
                def _(sync, e=e, bsrc=bsrc):
                    c.w(sync, "T")
                    c.inc(sync.dma_start(
                        out=bn16[:], in_=bsrc[e, :].rearrange(
                            "(a b) -> a b", b=128)), "DS", 16)

                @blk.tensor
                def _(t):
                    c.w(t, "DS", "V")
                    c.inc(t.transpose(psC[:, 0:16], bn16[:],
                                      cst["eye32"][0:16, 0:16]), "T")

                @blk.vector
                def _(v, e=e, bdst=bdst):
                    c.w(v, "T")
                    c.inc(v.tensor_copy(bdst[:, e * 16:(e + 1) * 16],
                                        psC[:, 0:16]), "V")

        for tc in range(NT // TC):
            t0 = tc * TC
            @blk.sync
            def _(sync, tc=tc, t0=t0):
                c.w(sync, "T", "V")
                for dt in range(DT):
                    c.inc(sync.dma_start(out=xc[:, dt * TC:(dt + 1) * TC],
                                         in_=xhat_in[dt, :, t0:t0 + TC]),
                          "DS", 16)

            for sub in range(TC // 512):
                @blk.tensor
                def _(t, tc=tc, sub=sub):
                    c.w(t, "DS", "V", "A")
                    for dt in range(DT):
                        c.inc(t.matmul(psC[0:8, 0:512],
                                       wgl[:, dt * 8:dt * 8 + 8],
                                       xc[:, dt * TC + sub * 512:
                                          dt * TC + sub * 512 + 512],
                                       start=(dt == 0), stop=(dt == DT - 1)),
                              "T")

                @blk.scalar
                def _(sc, sub=sub):
                    c.w(sc, "T")
                    c.inc(sc.copy(lgT[0:8, sub * 512:sub * 512 + 512],
                                  psC[0:8, 0:512]), "A")

            for g8 in range(TC // 128):
                @blk.tensor
                def _(t, g8=g8):
                    c.w(t, "A", "V")
                    c.inc(t.transpose(psC[:, 0:8],
                                      lgT[0:8, g8 * 128:g8 * 128 + 128],
                                      cst["eye32"][0:8, 0:8]), "T")

                @blk.vector
                def _(v, g8=g8):
                    c.w(v, "T")
                    c.inc(v.tensor_copy(lg[:], psC[:, 0:8]), "V")

                _gate_fix(c, blk, lg, ga, gb, gc, gm1, gm2, gm3, comb8,
                          comb8b, psC, cst, g8, tc)

            for e in range(NE):
                par_src = acA if e % 2 == 1 else acB
                par_dst = acB if e % 2 == 1 else acA
                for sub in range(TC // 512):
                    @blk.tensor
                    def _(t, e=e, sub=sub, tc=tc):
                        c.w(t, "V", "A")
                        c.inc(t.matmul(psRp[:, 0:512],
                                       cst["sel8"][:, e * 128:(e + 1) * 128],
                                       comb8b[0:8, sub * 512:sub * 512 + 512],
                                       start=True, stop=True), "T")

                    @blk.scalar
                    def _(sc, sub=sub):
                        c.w(sc, "T")
                        c.inc(sc.copy(combrep[:, sub * 512:sub * 512 + 512],
                                      psRp[:, 0:512]), "A")

                for fb in range(FT):
                    @blk.sync
                    def _(sync, e=e, fb=fb):
                        c.w(sync, "T")
                        c.inc(sync.dma_start(out=w12[:, 0:1024],
                                             in_=w1t[e, fb, :, :]), "DS", 16)
                        c.inc(sync.dma_start(out=w12[:, 1024:2048],
                                             in_=w2t[e, fb, :, :]), "DS", 16)
                    for sub in range(TC // 512):
                        @blk.tensor
                        def _(t, e=e, fb=fb, sub=sub, tc=tc):
                            c.w(t, "DS", "A", "V")
                            for dt in range(DT):
                                c.inc(t.matmul(
                                    psG[:, 0:512],
                                    w12[:, dt * 128:dt * 128 + 128],
                                    xc[:, dt * TC + sub * 512:dt * TC + sub * 512 + 512],
                                    start=(dt == 0), stop=(dt == DT - 1)), "T")
                            for dt in range(DT):
                                c.inc(t.matmul(
                                    psV[:, 0:512],
                                    w12[:, 1024 + dt * 128:1024 + dt * 128 + 128],
                                    xc[:, dt * TC + sub * 512:dt * TC + sub * 512 + 512],
                                    start=(dt == 0), stop=(dt == DT - 1)), "T")

                        @blk.scalar
                        def _(sc, e=e, fb=fb, sub=sub):
                            c.w(sc, "T")
                            c.inc(sc.activation(gsb[:], psG[:, 0:512], AF.Silu,
                                                bias=b1c[:, e * 16 + fb: e * 16 + fb + 1],
                                                scale=1.0), "A")

                        @blk.vector
                        def _(v, e=e, fb=fb, sub=sub):
                            c.w(v, "A", "T")
                            c.inc(v.tensor_mul(
                                gs2[:], gsb[:],
                                combrep[:, sub * 512:sub * 512 + 512]), "V")
                            c.w(v, "V")
                            c.inc(v.scalar_tensor_tensor(
                                hbf[:, fb * TC + sub * 512: fb * TC + sub * 512 + 512],
                                psV[:, 0:512],
                                b2c[:, e * 16 + fb: e * 16 + fb + 1],
                                gs2[:], ALU.add, ALU.mult), "V")

                for db in range(DT):
                    @blk.sync
                    def _(sync, e=e, db=db):
                        c.w(sync, "T")
                        c.inc(sync.dma_start(out=w3s[:, 0:2048],
                                             in_=w3t[e, db, :, :]), "DS", 16)
                    for sub in range(TC // 512):
                        @blk.tensor
                        def _(t, e=e, db=db, sub=sub, tc=tc):
                            c.w(t, "DS", "V", "A")
                            if e == 0:
                                c.inc(t.matmul(
                                    psY[:, 0:512],
                                    b3nat[:, db * 128:db * 128 + 128],
                                    comb8b[0:8, sub * 512:sub * 512 + 512],
                                    start=True, stop=False), "T")
                            for ft in range(FT):
                                c.inc(t.matmul(
                                    psY[:, 0:512],
                                    w3s[:, ft * 128:ft * 128 + 128],
                                    hbf[:, ft * TC + sub * 512:ft * TC + sub * 512 + 512],
                                    start=(e != 0 and ft == 0),
                                    stop=(ft == FT - 1)), "T")

                        sl = slice(db * TC + sub * 512, db * TC + sub * 512 + 512)
                        if e == 0:
                            @blk.vector
                            def _(v, sl=sl):
                                c.w(v, "T")
                                c.inc(v.tensor_copy(acA[:, sl], psY[:, 0:512]),
                                      "V")
                        else:
                            @blk.vector
                            def _(v, sl=sl, par_src=par_src, par_dst=par_dst):
                                c.w(v, "T")
                                c.inc(v.scalar_tensor_tensor(
                                    par_dst[:, sl], psY[:, 0:512], 1.0,
                                    par_src[:, sl], ALU.mult, ALU.add), "V")

            final_ac = acA if NE % 2 == 1 else acB
            for sub in range(TC // 512):
                s512 = t0 + sub * 512
                s, tloc = divmod(s512, S)
                @blk.sync
                def _(sync, s512=s512):
                    c.w(sync, "V", "A", "T")
                    for dt in range(DT):
                        c.inc(sync.dma_start(
                            out=arF2[:, dt * 256:dt * 256 + 256],
                            in_=rsrcs[0][dt, :, s512:s512 + 256]), "DS", 16)
                        c.inc(sync.dma_start(
                            out=arF[:, 4096 + dt * 256: 4096 + dt * 256 + 256],
                            in_=rsrcs[0][dt, :, s512 + 256:s512 + 512]),
                            "DS", 16)
                @blk.vector
                def _(v, sub=sub, s512=s512, final_ac=final_ac):
                    c.w(v, "DS", "V")
                    for dt in range(DT):
                        asl = slice(dt * TC + sub * 512, dt * TC + sub * 512 + 256)
                        asl2 = slice(dt * TC + sub * 512 + 256,
                                     dt * TC + sub * 512 + 512)
                        v.tensor_add(arF[:, dt * 512: dt * 512 + 256],
                                     final_ac[:, asl], arF2[:, dt * 256:dt * 256 + 256])
                        ins = v.tensor_add(arF[:, dt * 512 + 256: dt * 512 + 512],
                                           final_ac[:, asl2],
                                           arF[:, 4096 + dt * 256: 4096 + dt * 256 + 256])
                    c.inc(ins, "V")
                if len(rsrcs) > 1:
                    @blk.sync
                    def _(sync, s512=s512):
                        c.w(sync, "V")
                        for dt in range(DT):
                            c.inc(sync.dma_start(
                                out=arF2[:, dt * 256:dt * 256 + 256],
                                in_=rsrcs[1][dt, :, s512:s512 + 256]), "DS", 16)
                            c.inc(sync.dma_start(
                                out=arF[:, 4096 + dt * 256: 4096 + dt * 256 + 256],
                                in_=rsrcs[1][dt, :, s512 + 256:s512 + 512]),
                                "DS", 16)

                    @blk.vector
                    def _(v, sub=sub):
                        c.w(v, "DS", "V")
                        for dt in range(DT):
                            c.inc(v.tensor_add(
                                scrb[:, 0:256],
                                arF[:, dt * 512:dt * 512 + 256],
                                arF2[:, dt * 256:dt * 256 + 256]), "V")
                            c.w(v, "V")
                            c.inc(v.tensor_copy(
                                arF[:, dt * 512:dt * 512 + 256],
                                scrb[:, 0:256]), "V")
                            c.inc(v.tensor_add(
                                scrb[:, 256:512],
                                arF[:, dt * 512 + 256:dt * 512 + 512],
                                arF[:, 4096 + dt * 256:4096 + dt * 256 + 256]),
                                "V")
                            c.w(v, "V")
                            c.inc(v.tensor_copy(
                                arF[:, dt * 512 + 256:dt * 512 + 512],
                                scrb[:, 256:512]), "V")

                if y_nat is None:
                    emit_tail_chunk(c, cst, B, 8, arF[:, 0:4096],
                                    dict(ln=True, xhat="plain", store=True),
                                    s, (tloc // 512), xhat_out, rdst)
                else:
                    emit_tail_chunk(c, cst, B, 8, arF[:, 0:4096],
                                    dict(ln=True, xhat=None, store=False),
                                    s, (tloc // 512), None, None)
                    for g4 in range(4):
                        @blk.tensor
                        def _(t, g4=g4):
                            c.w(t, "V", "A")
                            for dt in range(DT):
                                ps = psG if dt < 4 else psV
                                c.inc(t.transpose(
                                    ps[:, (dt % 4) * 128:(dt % 4) * 128 + 128],
                                    arF[:, dt * 512 + g4 * 128: dt * 512 + g4 * 128 + 128],
                                    cst["eye32"][:]), "T")

                        @blk.scalar
                        def _(sc, g4=g4):
                            c.w(sc, "T")
                            sc.copy(natf[:, 0:512], psG[:, 0:512])
                            c.inc(sc.copy(natf[:, 512:1024], psV[:, 0:512]),
                                  "A")

                        @blk.sync
                        def _(sync, g4=g4, s=s, tloc=tloc):
                            c.w(sync, "A")
                            c.inc(sync.dma_start(
                                out=y_nat[s, tloc + g4 * 128: tloc + g4 * 128 + 128, :],
                                in_=natf[:]), "DS", 16)

        @blk.sync
        def _(sync):
            c.w(sync, "DS", "DG", "V", "A", "T")


def _gate_fix(c, blk, lg, ga, gb, gc, gm1, gm2, gm3, comb8, comb8b, psC,
              cst, g8, tc):
    @blk.vector
    def _(v):
        c.w(v, "V", "A")
        c.inc(v.reduce_max(gm1[:], lg[:], axis=AX.X, negate=True), "V")
        c.w(v, "V")
        c.inc(v.tensor_scalar(ga[:], lg[:], gm1[:], None, ALU.add), "V")
        c.w(v, "V")
        c.inc(v.tensor_single_scalar(gb[:], ga[:], 0.0, ALU.is_ge), "V")
        c.w(v, "V")
        c.inc(v.scalar_tensor_tensor(gc[:], gb[:], -1.0e38, lg[:], ALU.mult,
                                     ALU.add), "V")
        c.w(v, "V")
        c.inc(v.reduce_max(gm2[:], gc[:], axis=AX.X, negate=True), "V")

    @blk.scalar
    def _(sc):
        c.w(sc, "V")
        c.inc(sc.activation(ga[:], lg[:], AF.Exp, bias=gm1[:], scale=1.0),
              "A")

    @blk.vector
    def _(v):
        c.w(v, "A")
        c.inc(v.tensor_sub(gm3[:], gm1[:], gm2[:]), "V")

    @blk.scalar
    def _(sc):
        c.w(sc, "V")
        c.inc(sc.activation(gm1[:], gm3[:], AF.Exp), "A")

    @blk.vector
    def _(v, g8=g8):
        c.w(v, "A")
        c.inc(v.tensor_scalar(gm3[:], gm1[:], 1.0, None, ALU.add), "V")
        c.w(v, "V")
        c.inc(v.reciprocal(gm1[:], gm3[:]), "V")
        c.w(v, "V")
        c.inc(v.tensor_scalar(gc[:], lg[:], gm2[:], None, ALU.add), "V")
        c.w(v, "V")
        c.inc(v.tensor_single_scalar(gb[:], gc[:], 0.0, ALU.is_ge), "V")
        c.w(v, "V")
        c.inc(v.tensor_mul(gc[:], ga[:], gb[:]), "V")
        c.w(v, "V")
        c.inc(v.tensor_scalar(ga[:], gc[:], gm1[:], None, ALU.mult), "V")

    @blk.tensor
    def _(t, g8=g8):
        c.w(t, "V", "A")
        c.inc(t.transpose(psC[0:8, 0:128], ga[:, 0:8], cst["eye32"][:]), "T")

    @blk.vector
    def _(v, g8=g8):
        c.w(v, "T")
        v.tensor_copy(comb8[0:8, g8 * 128:g8 * 128 + 128], psC[0:8, 0:128])
        c.inc(v.tensor_copy(comb8b[0:8, g8 * 128:g8 * 128 + 128],
                            psC[0:8, 0:128]), "V")


def emit_linear_attention(c, cst, io, cfg, rsrc, rdst, xhat_in, xhat_out):
    blk, nc = c.blk, c.nc
    c.new_sems()
    name = cfg["name"]
    wqkv = io[cfg["wqkv"]]
    wproj = io[cfg["wproj"]]

    _st = c.stack.__class__()
    with _st:
        xbf = _st.enter_context(nc.sbuf_tensor(f"{name}_xbf", [128, DT * S], BF16))
        aT = _st.enter_context(nc.sbuf_tensor(f"{name}_aT", [128, DT * S], BF16))
        qT = _st.enter_context(nc.sbuf_tensor(f"{name}_qT", [128, S], BF16))
        kT = _st.enter_context(nc.sbuf_tensor(f"{name}_kT", [128, S], BF16))
        vb = _st.enter_context(nc.sbuf_tensor(f"{name}_vb", [128, S], BF16))
        qb = _st.enter_context(nc.sbuf_tensor(f"{name}_qb", [128, S], BF16))
        kb = _st.enter_context(nc.sbuf_tensor(f"{name}_kb", [128, S], BF16))
        qf = _st.enter_context(nc.sbuf_tensor(f"{name}_qf", [128, S], BF16))
        kf = _st.enter_context(nc.sbuf_tensor(f"{name}_kf", [128, S], BF16))
        kfnat = _st.enter_context(nc.sbuf_tensor(f"{name}_kfn", [128, S], BF16))
        vnat = _st.enter_context(nc.sbuf_tensor(f"{name}_vn", [128, S], BF16))
        tmp1 = _st.enter_context(nc.sbuf_tensor(f"{name}_t1", [128, S], BF16))
        tmp2 = _st.enter_context(nc.sbuf_tensor(f"{name}_t2", [128, S], BF16))
        tmp3 = _st.enter_context(nc.sbuf_tensor(f"{name}_t3", [128, S], F32))
        rrep = _st.enter_context(nc.sbuf_tensor(f"{name}_rrep", [128, S], F32))
        arF = _st.enter_context(nc.sbuf_tensor(f"{name}_arF", [128, 6144], F32))
        wbuf = _st.enter_context(nc.sbuf_tensor(f"{name}_wb", [128, 8192], BF16))
        natb = _st.enter_context(nc.sbuf_tensor(f"{name}_nat", [128, 1024], BF16))
        kvsb = _st.enter_context(nc.sbuf_tensor(f"{name}_kv", [128, 64], BF16))
        kscol = _st.enter_context(nc.sbuf_tensor(f"{name}_ks", [128, 1], F32))
        kscolb = _st.enter_context(nc.sbuf_tensor(f"{name}_ksb", [128, 1], BF16))
        row1 = _st.enter_context(nc.sbuf_tensor(f"{name}_rw1", [1, 512], F32))
        row2 = _st.enter_context(nc.sbuf_tensor(f"{name}_rw2", [1, 512], F32))
        row3 = _st.enter_context(nc.sbuf_tensor(f"{name}_rw3", [1, 512], F32))
        row4 = _st.enter_context(nc.sbuf_tensor(f"{name}_rw4", [1, 512], F32))
        xw = _st.enter_context(nc.sbuf_tensor(f"{name}_xw", [128, 8], F32))
        psA = _st.enter_context(nc.psum_tensor(f"{name}_psA", [128, 512], F32))
        psB = _st.enter_context(nc.psum_tensor(f"{name}_psB", [128, 512], F32))
        psPT = _st.enter_context(nc.psum_tensor(f"{name}_psPT", [128, 1024], BF16))
        psKV = _st.enter_context(nc.psum_tensor(f"{name}_psKV", [64, 512], F32))
        psR = _st.enter_context(nc.psum_tensor(f"{name}_psR", [1, 512], F32))
        psRp = _st.enter_context(nc.psum_tensor(f"{name}_psRp", [128, 512], F32))
        B = dict(tb1=tmp1, tb2=tmp2, row1=row1, row2=row2, row3=row3,
                 row4=row4, trep1=rrep[:, 0:512], trep2=rrep[:, 512:1024],
                 scr=rrep[:, 1024:1536], psR=psR, psRp=psRp, xw=xw)
        u_view = arF[:, 0:4096]

        for s in range(NSEQ):
            @blk.sync
            def _(sync, s=s):
                c.w(sync, "T", "V", "A")
                for dt in range(DT):
                    c.inc(sync.dma_start(out=xbf[:, dt * S:(dt + 1) * S],
                                         in_=xhat_in[dt, :, s * S:(s + 1) * S]),
                          "DS", 16)
            for p in range(NH // 2):
                for sec in range(3):
                    @blk.gpsimd
                    def _(g, sec=sec, p=p):
                        c.w(g, "T", "V")
                        c.inc(g.dma_start(
                            out=natb[:],
                            in_=wqkv[sec * D + p * 128: sec * D + (p + 1) * 128, :]),
                            "DG", 16)

                    @blk.tensor
                    def _(t):
                        c.w(t, "DG", "V")
                        for dt in range(DT):
                            c.inc(t.transpose(psPT[:, dt * 128:dt * 128 + 128],
                                              natb[:, dt * 128:dt * 128 + 128],
                                              cst["eyeb"][:]), "T")

                    @blk.vector
                    def _(v, sec=sec):
                        c.w(v, "T")
                        c.inc(v.tensor_copy(wbuf[:, sec * 1024:(sec + 1) * 1024],
                                            psPT[:]), "V")

                for sec, dst in ((0, qT), (1, kT), (2, vb)):
                    for tcq in range(4):
                        ps = psA if tcq % 2 == 0 else psB
                        @blk.tensor
                        def _(t, sec=sec, tcq=tcq, ps=ps, s=s):
                            c.w(t, "V", "A", "DS")
                            for dt in range(DT):
                                c.inc(t.matmul(
                                    ps[:, 0:512],
                                    wbuf[:, sec * 1024 + dt * 128:sec * 1024 + dt * 128 + 128],
                                    xbf[:, dt * S + tcq * 512:
                                        dt * S + tcq * 512 + 512],
                                    start=(dt == 0), stop=(dt == DT - 1)), "T")

                        @blk.scalar
                        def _(sc, tcq=tcq, ps=ps, dst=dst):
                            c.w(sc, "T")
                            c.inc(sc.activation(dst[:, tcq * 512:tcq * 512 + 512],
                                                ps[:, 0:512], AF.Copy), "A")

                for src, dst in ((qT, qb), (kT, kb)):
                    @blk.vector
                    def _(v, src=src, dst=dst):
                        c.w(v, "A", "T")
                        v.tensor_mul(tmp1[:], src[:], cst["cosb"][:])
                        v.tensor_mul(tmp2[0:32, :], src[32:64, :], cst["sinb"][32:64, :])
                        v.tensor_mul(tmp2[32:64, :], src[0:32, :], cst["sinb"][0:32, :])
                        v.tensor_mul(tmp2[64:96, :], src[96:128, :], cst["sinb"][96:128, :])
                        c.inc(v.tensor_mul(tmp2[96:128, :], src[64:96, :], cst["sinb"][64:96, :]), "V")
                        c.w(v, "V")
                        c.inc(v.tensor_sub(dst[:], tmp1[:], tmp2[:]), "V")

                for src, dst, scl in ((qb, qf, 0.125), (kb, kf, 1.0)):
                    @blk.vector
                    def _(v, src=src, dst=dst, scl=scl):
                        c.w(v, "V")
                        c.inc(v.tensor_scalar(tmp3[:, 0:2048], src[:], scl, 0.0,
                                              ALU.mult, ALU.min), "V")

                    @blk.scalar
                    def _(sc):
                        c.w(sc, "V")
                        c.inc(sc.activation(tmp1[:], tmp3[:, 0:2048], AF.Exp),
                              "A")

                    @blk.vector
                    def _(v, src=src, dst=dst, scl=scl):
                        c.w(v, "A")
                        c.inc(v.tensor_scalar(tmp2[:], src[:], scl, 0.0,
                                              ALU.mult, ALU.max), "V")
                        c.w(v, "V")
                        c.inc(v.tensor_add(dst[:], tmp1[:], tmp2[:]), "V")

                @blk.vector
                def _(v):
                    c.w(v, "V")
                    c.inc(v.reduce_sum(kscol[:], kf[:], axis=AX.X), "V")
                    c.w(v, "V")
                    c.inc(v.tensor_copy(kscolb[:], kscol[:]), "V")

                for buf, dst in ((kf, kfnat), (vb, vnat)):
                    for hh in range(2):
                        b0 = hh * 64
                        @blk.tensor
                        def _(t, buf=buf, b0=b0):
                            c.w(t, "V", "A")
                            for j in range(16):
                                c.inc(t.transpose(
                                    psPT[:, j * 64:j * 64 + 64],
                                    buf[b0:b0 + 64, j * 128:j * 128 + 128],
                                    cst["eyeb"][b0:b0 + 64, b0:b0 + 64]), "T")

                        @blk.vector
                        def _(v, dst=dst, hh=hh):
                            c.w(v, "T")
                            c.inc(v.tensor_copy(dst[:, hh * 1024:(hh + 1) * 1024],
                                                psPT[:]), "V")

                for hh in range(2):
                    b0 = hh * 64
                    @blk.tensor
                    def _(t, hh=hh, b0=b0):
                        c.w(t, "V", "A")
                        for j in range(16):
                            c.inc(t.matmul(
                                psKV[0:64, 0:64],
                                kfnat[:, hh * 1024 + j * 64:hh * 1024 + j * 64 + 64],
                                vnat[:, hh * 1024 + j * 64:hh * 1024 + j * 64 + 64],
                                start=(j == 0), stop=(j == 15)), "T")

                    @blk.vector
                    def _(v, b0=b0):
                        c.w(v, "T")
                        c.inc(v.tensor_copy(kvsb[b0:b0 + 64, :],
                                            psKV[0:64, 0:64]), "V")

                    for tcq in range(4):
                        @blk.tensor
                        def _(t, b0=b0, tcq=tcq):
                            c.w(t, "V", "A")
                            c.inc(t.matmul(psR[0:1, 0:512],
                                           kscolb[b0:b0 + 64, :],
                                           qf[b0:b0 + 64, tcq * 512:tcq * 512 + 512],
                                           start=True, stop=True), "T")

                        @blk.scalar
                        def _(sc, tcq=tcq):
                            c.w(sc, "T")
                            c.inc(sc.copy(row1[0:1, 0:512],
                                          psR[0:1, 0:512]), "A")

                        @blk.vector
                        def _(v):
                            c.w(v, "A")
                            c.inc(v.tensor_scalar(row2[:], row1[:], 1e-6,
                                                  None, ALU.max), "V")
                            c.w(v, "V")
                            c.inc(v.reciprocal(row3[:], row2[:]), "V")

                        @blk.tensor
                        def _(t, tcq=tcq):
                            c.w(t, "V", "A")
                            c.inc(t.matmul(psRp[0:64, 0:512],
                                           cst["ones1"][0:1, 0:64],
                                           row3[0:1, 0:512],
                                           start=True, stop=True), "T")

                        @blk.scalar
                        def _(sc, tcq=tcq):
                            c.w(sc, "T")
                            c.inc(sc.copy(rrep[0:64, tcq * 512:tcq * 512 + 512],
                                          psRp[0:64, 0:512]), "A")

                    for tcq in range(4):
                        @blk.tensor
                        def _(t, b0=b0, tcq=tcq):
                            c.w(t, "V", "A")
                            c.inc(t.matmul(psKV[0:64, 0:512],
                                           kvsb[b0:b0 + 64, :],
                                           qf[b0:b0 + 64, tcq * 512:tcq * 512 + 512],
                                           start=True, stop=True), "T")

                        @blk.vector
                        def _(v, b0=b0, tcq=tcq, p=p, s=s):
                            c.w(v, "T", "A")
                            c.inc(v.tensor_mul(
                                aT[b0:b0 + 64, p * S + tcq * 512: p * S + tcq * 512 + 512],
                                psKV[0:64, 0:512],
                                rrep[0:64, tcq * 512:tcq * 512 + 512]), "V")

            for ob in range(8):
                @blk.gpsimd
                def _(g, ob=ob):
                    c.w(g, "T", "V")
                    c.inc(g.dma_start(out=natb[:],
                                      in_=wproj[ob * 128:(ob + 1) * 128, :]),
                          "DG", 16)

                @blk.tensor
                def _(t):
                    c.w(t, "DG", "V")
                    for dt in range(DT):
                        c.inc(t.transpose(psPT[:, dt * 128:dt * 128 + 128],
                                          natb[:, dt * 128:dt * 128 + 128],
                                          cst["eyeb"][:]), "T")

                @blk.vector
                def _(v, ob=ob):
                    c.w(v, "T")
                    c.inc(v.tensor_copy(wbuf[:, ob * 1024:(ob + 1) * 1024],
                                        psPT[:]), "V")

            for tc in range(4):
                for half in range(2):
                    @blk.sync
                    def _(sync, tc=tc, half=half, s=s):
                        c.w(sync, "V", "A", "T")
                        for dt in range(DT):
                            c.inc(sync.dma_start(
                                out=arF[:, 4096 + dt * 256: 4096 + dt * 256 + 256],
                                in_=rsrc[dt, :, s * S + tc * 512 + half * 256:
                                         s * S + tc * 512 + half * 256 + 256]),
                                "DS", 16)
                    for ob in range(8):
                        ps = psA if ob % 2 == 0 else psB
                        @blk.tensor
                        def _(t, tc=tc, ob=ob, half=half, ps=ps, s=s):
                            c.w(t, "V", "A", "DS")
                            for adt in range(DT):
                                c.inc(t.matmul(
                                    ps[:, 0:256],
                                    wbuf[:, ob * 1024 + adt * 128:ob * 1024 + adt * 128 + 128],
                                    aT[:, adt * S + tc * 512 + half * 256:
                                       adt * S + tc * 512 + half * 256 + 256],
                                    start=(adt == 0), stop=(adt == DT - 1)), "T")

                        @blk.vector
                        def _(v, ob=ob, half=half, ps=ps):
                            c.w(v, "T", "DS")
                            c.inc(v.scalar_tensor_tensor(
                                u_view[:, ob * 512 + half * 256:
                                       ob * 512 + half * 256 + 256],
                                ps[:, 0:256], 0.0,
                                arF[:, 4096 + ob * 256: 4096 + ob * 256 + 256],
                                ALU.add, ALU.add), "V")
                emit_tail_chunk(c, cst, B, 8, u_view,
                                dict(ln=True, xhat="plain", store=True),
                                s, tc, xhat_out, rdst)

        @blk.sync
        def _(sync):
            c.w(sync, "DS", "DG", "V", "A", "T")


L0_ATTN_CFG = dict(name="a0", wqkv="L0_attn_qkv_w", bqkv=None,
                   wproj="L0_attn_proj_w", bproj="L0_attn_proj_b",
                   causal=True, rot_sign="A",
                   ptypes=["sdpa"] * 3 + ["hyp"] * 2 + ["cos"] * 3,
                   ln=True, xhat="rms", rms_w="L0_ffn_norm1_w")
L0_FFN_ATTN_CFG = dict(name="a1", wqkv="L0_ffn_qkv_w", bqkv="L0_ffn_qkv_b",
                       wproj="L0_ffn_out_w", bproj="L0_ffn_out_b",
                       causal=False, rot_sign="B", ptypes=["sdpa"] * 8,
                       ln=False, xhat="rms", rms_w="L0_ffn_norm2_w")
MOE0_CFG = dict(name="m0", wg="L0_ffn_gate_w", w1="L0_ffn_w1", b1="L0_ffn_b1",
                w2="L0_ffn_w2", b2="L0_ffn_b2", w3="L0_ffn_w3", b3="L0_ffn_b3")
L1_ATTN_CFG = dict(name="la", wqkv="L1_attn_w_qkv", wproj="L1_attn_w_out")
MOE1_CFG = dict(name="m1", wg="L1_ffn_Wg", w1="L1_ffn_W1", b1="L1_ffn_b1",
                w2="L1_ffn_W2", b2="L1_ffn_b2", w3="L1_ffn_W3", b3="L1_ffn_b3")

INPUT_SPECS = [
    ("L0_attn_qkv_w", (3072, 1024)), ("L0_attn_proj_w", (1024, 1024)),
    ("L0_attn_proj_b", (1024,)), ("L0_ffn_norm1_w", (1024,)),
    ("L0_ffn_norm2_w", (1024,)), ("L0_ffn_qkv_w", (3072, 1024)),
    ("L0_ffn_qkv_b", (3072,)), ("L0_ffn_out_w", (1024, 1024)),
    ("L0_ffn_out_b", (1024,)), ("L0_ffn_gate_w", (8, 1024)),
    ("L0_ffn_w1", (8, 2048, 1024)), ("L0_ffn_b1", (8, 2048)),
    ("L0_ffn_w2", (8, 2048, 1024)), ("L0_ffn_b2", (8, 2048)),
    ("L0_ffn_w3", (8, 1024, 2048)), ("L0_ffn_b3", (8, 1024)),
    ("L1_attn_w_qkv", (3072, 1024)), ("L1_attn_w_out", (1024, 1024)),
    ("L1_ffn_Wg", (8, 1024)), ("L1_ffn_W1", (8, 2048, 1024)),
    ("L1_ffn_b1", (8, 2048)), ("L1_ffn_W2", (8, 2048, 1024)),
    ("L1_ffn_b2", (8, 2048)), ("L1_ffn_W3", (8, 1024, 2048)),
    ("L1_ffn_b3", (8, 1024)),
]


def build_program(phases):
    import contextlib
    nc = bass.Bass("TRN2", target_bir_lowering=False, debug=False)
    io = {}
    io["x"] = nc.dram_tensor("x", [NSEQ, S, D], F32, kind="ExternalInput").ap()
    for nm, shp in INPUT_SPECS:
        io[nm] = nc.dram_tensor(nm, list(shp), F32, kind="ExternalInput").ap()
    for nm, shp, dt in AUX_SPECS:
        io[nm] = nc.dram_tensor(nm, list(shp),
                                BF16 if dt is BF else F32,
                                kind="ExternalInput").ap()
    io["curv_tok"] = nc.dram_tensor("curv_tok", [4, 128], F32,
                                    kind="ExternalInput").ap()
    y = nc.dram_tensor("y", [NSEQ, S, D], F32, kind="ExternalOutput").ap()
    if DBG["on"]:
        DBG["io"] = {}
        for nm in ("qT", "kT", "vb", "qb", "kb", "Ssb", "Pn", "aT",
                   "aT3", "aT5", "u0a", "u0b", "t_mu", "t_msq", "t_var",
                   "t_rstd", "t_rep1", "t_rep2"):
            dtb = (F32 if nm in ("Ssb", "u0a", "u0b") or nm.startswith("t_")
                   else BF16)
            DBG["io"][nm] = nc.dram_tensor(
                "dbg_" + nm, [128, 2048], dtb, kind="ExternalOutput").ap()
    rA = nc.dram_tensor("rA", [DT, 128, NT], F32, kind="Internal").ap()
    rB = nc.dram_tensor("rB", [DT, 128, NT], F32, kind="Internal").ap()
    xhat = nc.dram_tensor("xhat", [DT, 128, NT], BF16, kind="Internal").ap()
    w1t0 = nc.dram_tensor("w1t0", [NE, FT, 128, 1024], BF16, kind="Internal").ap()
    w2t0 = nc.dram_tensor("w2t0", [NE, FT, 128, 1024], BF16, kind="Internal").ap()
    w3t0 = nc.dram_tensor("w3t0", [NE, DT, 128, 2048], BF16, kind="Internal").ap()

    with contextlib.ExitStack() as stack:
        blk = stack.enter_context(nc.Block())
        c = Ctx(nc, stack, blk)
        cst = alloc_consts(c)
        with nc.sbuf_tensor("c_scratch", [128, 4096], F32) as scrF:
            load_consts(c, io, cst, scrF)

        for ph in phases:
            if ph == "in":
                emit_in_phase(c, cst, io["x"], rA, xhat)
            elif ph == "in_rms1":
                emit_in_phase(c, cst, io["x"], rB, xhat)
                emit_rms_rewrite(c, cst, io, rB, xhat, "L0_ffn_norm1_w")
            elif ph == "a0":
                emit_softmax_attention(c, cst, io, L0_ATTN_CFG, rA, rB,
                                       xhat, xhat)
            elif ph == "a1":
                emit_softmax_attention(c, cst, io, L0_FFN_ATTN_CFG, rB, rA,
                                       xhat, xhat)
            elif ph == "m0":
                cfgd = dict(MOE0_CFG)
                cfgd.update(w1t=w1t0, w2t=w2t0, w3t=w3t0)
                emit_moe(c, cst, io, cfgd, [rA, rB], rA, xhat, xhat)
            elif ph == "la":
                emit_linear_attention(c, cst, io, L1_ATTN_CFG, rA, rA,
                                      xhat, xhat)
            elif ph == "m1":
                cfgd = dict(MOE1_CFG)
                cfgd.update(w1t=w1t0, w2t=w2t0, w3t=w3t0)
                emit_moe(c, cst, io, cfgd, [rA], rA, xhat, xhat, y_nat=y)
            elif ph == "store_a":
                emit_store_phase(c, cst, rA, y)
            elif ph == "store_b":
                emit_store_phase(c, cst, rB, y)
    return nc


def emit_rms_rewrite(c, cst, io, rsrc, xhat, wname):
    blk, nc = c.blk, c.nc
    c.new_sems()
    _st = c.stack.__class__()
    with _st:
        ub = _st.enter_context(nc.sbuf_tensor("rr_u", [128, 4096], F32))
        sqb = _st.enter_context(nc.sbuf_tensor("rr_sq", [128, 4096], BF16))
        ob = _st.enter_context(nc.sbuf_tensor("rr_o", [128, 4096], BF16))
        row = _st.enter_context(nc.sbuf_tensor("rr_rw", [1, 512], F32))
        row2 = _st.enter_context(nc.sbuf_tensor("rr_rw2", [1, 512], F32))
        rep = _st.enter_context(nc.sbuf_tensor("rr_rep", [128, 512], F32))
        scr = _st.enter_context(nc.sbuf_tensor("rr_scr", [128, 512], F32))
        xw = _st.enter_context(nc.sbuf_tensor("rr_xw", [128, 8], F32))
        psR = _st.enter_context(nc.psum_tensor("rr_psR", [1, 512], F32))
        psRp = _st.enter_context(nc.psum_tensor("rr_psRp", [128, 512], F32))
        B = dict(psRp=psRp, xw=xw)
        load_norm_w_cols(c, cst, B, io[wname])
        for ch in range(NT // 512):
            @blk.sync
            def _(sync, ch=ch):
                c.w(sync, "V", "A")
                for dt in range(DT):
                    c.inc(sync.dma_start(
                        out=ub[:, dt * 512:dt * 512 + 512],
                        in_=rsrc[dt, :, ch * 512:(ch + 1) * 512]), "DS", 16)

            @blk.vector
            def _(v):
                c.w(v, "DS", "A")
                c.inc(v.tensor_mul(sqb[:], ub[:], ub[:]), "V")

            @blk.tensor
            def _(t):
                c.w(t, "V", "A")
                for dt in range(DT):
                    c.inc(t.matmul(psR[0:1, 0:512], cst["onescol_b"][:],
                                   sqb[:, dt * 512:dt * 512 + 512],
                                   start=(dt == 0), stop=(dt == DT - 1)), "T")

            @blk.scalar
            def _(sc):
                c.w(sc, "T")
                c.inc(sc.activation(row[0:1, :], psR[0:1, 0:512], AF.Ln,
                                    bias=cst["fcol"][0:1, 2:3], scale=1.0 / D),
                      "A")
                c.w(sc, "A")
                c.inc(sc.activation(row2[0:1, :], row[0:1, :], AF.Exp,
                                    bias=0.0, scale=-0.5), "A")

            @blk.tensor
            def _(t):
                c.w(t, "A", "V")
                c.inc(t.matmul(psRp[:, 0:512], cst["ones1"][:], row2[0:1, :],
                               start=True, stop=True), "T")

            @blk.vector
            def _(v):
                c.w(v, "T")
                c.inc(v.tensor_copy(rep[:, 0:512], psRp[:, 0:512]), "V")

            @blk.vector
            def _(v):
                for dt in range(DT):
                    c.inc(v.tensor_mul(scr[:, 0:512],
                                       ub[:, dt * 512:dt * 512 + 512],
                                       rep[:, 0:512]), "V")
                    c.w(v, "V")
                    c.inc(v.tensor_scalar(ob[:, dt * 512:dt * 512 + 512],
                                          scr[:, 0:512], xw[:, dt:dt + 1],
                                          None, ALU.mult), "V")
                    c.w(v, "V")

            @blk.sync
            def _(sync, ch=ch):
                c.w(sync, "V")
                for dt in range(DT):
                    c.inc(sync.dma_start(
                        out=xhat[dt, :, ch * 512:(ch + 1) * 512],
                        in_=ob[:, dt * 512:dt * 512 + 512]), "DS", 16)

        @blk.sync
        def _(sync):
            c.w(sync, "DS")


_LAUNCH_PLANS = {
    "fused": [["in", "a0", "a1", "m0", "la", "m1"]],
    "unfused": [["in", "a0", "store_b"],
                ["in_rms1", "a1", "m0", "store_a"],
                ["in", "la", "store_a"],
                ["in", "m1"]],
}
KERNEL_MODE = "unfused"


def _core_inputs(inputs, aux, core):
    d = {}
    d["x"] = np.ascontiguousarray(inputs["L0_attn_x"][2 * core: 2 * core + 2])
    for nm, shp in INPUT_SPECS:
        d[nm] = np.ascontiguousarray(inputs[nm], dtype=np.float32)
    for nm, shp, dt in AUX_SPECS:
        d[nm] = aux[nm]
    d["curv_tok"] = np.ascontiguousarray(
        np.broadcast_to(inputs["L0_attn_curvature"].reshape(4, 1),
                        (4, 128)).astype(np.float32))
    return d


def kernel(**inputs):
    aux = host_aux()
    plans = _LAUNCH_PLANS[KERNEL_MODE]
    n = 8
    cur_x = [np.ascontiguousarray(
        np.asarray(inputs["L0_attn_x"], dtype=np.float32))]

    outs = None
    for li, phases in enumerate(plans):
        nc = build_program(phases)
        in_maps = []
        for core in range(n):
            m = _core_inputs(inputs, aux, core)
            m["x"] = np.ascontiguousarray(cur_x[0][2 * core: 2 * core + 2])
            in_maps.append(m)
        res = run_bass_kernel_spmd(nc, in_maps, core_ids=list(range(n)))
        ycat = np.concatenate([np.asarray(res.results[i]["y"])
                               for i in range(n)], axis=0)
        cur_x = [ycat.reshape(16, S, D)]
    return cur_x[0].astype(np.float32)
